# version 02
# baseline (speedup 1.0000x reference)
import numpy as np
import ml_dtypes
from contextlib import ExitStack

import concourse.bass as bass
import concourse.mybir as mybir
from concourse.bass_utils import run_bass_kernel_spmd

F32 = mybir.dt.float32
BF16 = mybir.dt.bfloat16
F16 = mybir.dt.float16
I32 = mybir.dt.int32
U16 = mybir.dt.uint16
AF = mybir.ActivationFunctionType
ALU = mybir.AluOpType
AX = mybir.AxisListType

NCORES = 8
T = 8192
TL = T // NCORES
D = 2048
HD = 128
HLOC = 2
I_FF = 5632
QK_SCALE = float(1.0 / np.sqrt(HD))
MASKVAL = -100.0
CH = 32
NCH = TL // CH
KB = 64
VB = 128

STAGE = 4
DEBUG = False

_GS = [0]


def gs(name):
    _GS[0] += 1
    return f"{name}_{_GS[0]}"


class Chan:
    def __init__(self, nc, ctx, name):
        self.nc = nc
        self.ctx = ctx
        self.name = name
        self.sem = ctx.enter_context(nc.semaphore(name))
        self.n = 0

    def rotate(self):
        self.n = 0

    def inc(self, instr, k=1):
        instr.then_inc(self.sem, k)
        self.n += k
        return self.n

    def dinc(self, instr):
        return self.inc(instr, 16)

    def wait(self, eng, at=None):
        v = self.n if at is None else at
        if v > 0:
            eng.wait_ge(self.sem, v)


def build_program():
    nc = bass.Bass("TRN2", target_bir_lowering=False)

    def din(name, shape, dt):
        return nc.dram_tensor(name, shape, dt, kind="ExternalInput")

    x_in = din("x_in", [TL, D], F32)
    wq_s = din("wq_s", [D, 2 * HD], F32)
    wk_s = din("wk_s", [D, 2 * HD], F32)
    wv_s = din("wv_s", [D, 2 * HD], F32)
    wo_s = din("wo_s", [2 * HD, D], F32)
    gup_s = din("gup_s", [512, 1408], U16)
    gup_sc = din("gup_sc", [16, 1408], F16)
    dwn_s = din("dwn_s", [1408, 256], U16)
    dwn_sc = din("dwn_sc", [44, 256], F16)
    bwq = din("bwq", [D, KB], F32)
    bwk = din("bwk", [D, KB], F32)
    bwv = din("bwv", [D, VB], F32)
    bwo = din("bwo", [VB, D], F32)
    bwb = din("bwb", [D, KB], F32)
    bbb = din("bbb", [KB, 1], F32)
    bwd = din("bwd", [D, KB], F32)
    bbd = din("bbd", [KB, 1], F32)
    smask_in = din("smask_in", [128, 1024], F32)
    cmask_in = din("cmask_in", [CH, CH], BF16)
    r1hot_in = din("r1hot_in", [KB, 8 * VB], F32)
    rk_in = din("rk_in", [1, 1], I32)
    onesc_in = din("onesc_in", [128, 1], F32)
    onesb_in = din("onesb_in", [128, 1], BF16)
    onesr_in = din("onesr_in", [1, 128], F32)
    ind_in = din("ind_in", [1, 512], F32)
    ident_in = din("ident_in", [128, 128], BF16)
    eps_in = din("eps_in", [128, 2], F32)

    y_out = nc.dram_tensor("y_out", [TL, D], F32, kind="ExternalOutput")
    if DEBUG:
        dbg_xn = nc.dram_tensor("dbg_xn", [TL, D], BF16, kind="ExternalOutput")
        dbg_q = nc.dram_tensor("dbg_q", [2 * HD, T], BF16, kind="ExternalOutput")
        dbg_k = nc.dram_tensor("dbg_k", [2 * HD, T], BF16, kind="ExternalOutput")
        dbg_v = nc.dram_tensor("dbg_v", [T, 2 * HD], BF16, kind="ExternalOutput")
        dbg_z = nc.dram_tensor("dbg_z", [2 * HD, T], BF16, kind="ExternalOutput")
        dbg_zl = nc.dram_tensor("dbg_zl", [D, TL], BF16, kind="ExternalOutput")
        dbg_nb = nc.dram_tensor("dbg_nb", [128, 2], F32, kind="ExternalOutput")
        dbg_wqt = nc.dram_tensor("dbg_wqt", [D, 2 * HD], BF16, kind="ExternalOutput")
        dbg_st = nc.dram_tensor("dbg_st", [128, 8], F32, kind="ExternalOutput")
        dbg_nrm = nc.dram_tensor("dbg_nrm", [1, 16], F32, kind="ExternalOutput")
        dbg_wo = nc.dram_tensor("dbg_wo", [256, D], BF16, kind="ExternalOutput")
        dbg_wob = nc.dram_tensor("dbg_wob", [D, 256], BF16, kind="ExternalOutput")
        dbg_woi = nc.dram_tensor("dbg_woi", [256, D], BF16, kind="ExternalOutput")
        dbg_gu = nc.dram_tensor("dbg_gu", [256, 1408], BF16, kind="ExternalOutput")
        dbg_gui = nc.dram_tensor("dbg_gui", [256, 1408], BF16, kind="ExternalOutput")
        dbg_dw = nc.dram_tensor("dbg_dw", [256, 256], BF16, kind="ExternalOutput")
        dbg_h = nc.dram_tensor("dbg_h", [I_FF, TL], BF16, kind="ExternalOutput")
        dbg_y = nc.dram_tensor("dbg_y", [128, D], F32, kind="ExternalOutput")
        dbg_rst = nc.dram_tensor("dbg_rst", [128, 48], F32, kind="ExternalOutput")
        dbg_bo = nc.dram_tensor("dbg_bo", [128, TL], F32, kind="ExternalOutput")
        dbg_x3 = nc.dram_tensor("dbg_x3", [TL, D], F32, kind="ExternalOutput")
        dbg_smy = nc.dram_tensor("dbg_smy", [KB, VB], F32, kind="ExternalOutput")
        dbg_tr = nc.dram_tensor("dbg_tr", [NCORES * (KB + 1), VB], F32,
                                kind="ExternalOutput")
        dbg_blb = nc.dram_tensor("dbg_blb", [KB, TL], F32, kind="ExternalOutput")
        dbg_bq2 = nc.dram_tensor("dbg_bq2", [KB, TL], F32, kind="ExternalOutput")
        dbg_scr = nc.dram_tensor("dbg_scr", [1, 16 * 1408], F32, kind="ExternalOutput")
        dbg_s128 = nc.dram_tensor("dbg_s128", [128, 1408], F32, kind="ExternalOutput")
        dbg_m = nc.dram_tensor("dbg_m", [128, 1408], U16, kind="ExternalOutput")
        dbg_a = nc.dram_tensor("dbg_a", [128, 1408], U16, kind="ExternalOutput")
        dbg_pat = nc.dram_tensor("dbg_pat", [128, 1408], U16, kind="ExternalOutput")
        dbg_wt = nc.dram_tensor("dbg_wt", [128, 1408], BF16, kind="ExternalOutput")
        dbg_pk = nc.dram_tensor("dbg_pk", [128, 1408], U16, kind="ExternalOutput")

    def dint(name, shape, dt):
        return nc.dram_tensor(name, shape, dt)

    xn_tok_d = dint("xn_tok_d", [TL, D], BF16)
    w_bf_d = dint("w_bf_d", [2 * HD, D], BF16)
    wqT_d = dint("wqT_d", [D, 2 * HD], BF16)
    wkT_d = dint("wkT_d", [D, 2 * HD], BF16)
    wvT_d = dint("wvT_d", [D, 2 * HD], BF16)
    wo_bf_d = dint("wo_bf_d", [D, 2 * HD], BF16)
    agxn_in = dint("agxn_in", [D, TL], BF16)
    agxn_out = dint("agxn_out", [NCORES * D, TL], BF16)
    agwo_in = dint("agwo_in", [2 * HD, D], BF16)
    agwo_out = dint("agwo_out", [NCORES * 2 * HD, D], BF16)
    aggu_in = dint("aggu_in", [D, 1408], BF16)
    aggu_out = dint("aggu_out", [NCORES * D, 1408], BF16)
    agdw_in = dint("agdw_in", [I_FF, 256], BF16)
    agdw_out = dint("agdw_out", [NCORES * I_FF, 256], BF16)
    q_d = dint("q_d", [2 * HD, T], BF16)
    k_d = dint("k_d", [2 * HD, T], BF16)
    v_d = dint("v_d", [T, 2 * HD], BF16)
    agz_in = dint("agz_in", [2 * HD, T], BF16)
    agz_out = dint("agz_out", [NCORES * 2 * HD, T], BF16)
    zloc_d = dint("zloc_d", [D, TL], BF16)
    xnf_d = dint("xnf_d", [D, TL], BF16)
    h_d = dint("h_d", [I_FF, TL], BF16)
    agtr_in = dint("agtr_in", [KB + 1, VB], F32)
    agtr_out = dint("agtr_out", [NCORES * (KB + 1), VB], F32)

    ctx = ExitStack()
    with ctx:
        LD = Chan(nc, ctx, "LD")
        LD2 = Chan(nc, ctx, "LD2")
        LDG = Chan(nc, ctx, "LDG")
        PE = Chan(nc, ctx, "PE")
        DV = Chan(nc, ctx, "DV")
        AC = Chan(nc, ctx, "AC")
        PL = Chan(nc, ctx, "PL")
        ST = Chan(nc, ctx, "ST")
        CCs = {}
        CHANS = [LD, LD2, LDG, PE, DV, AC, PL, ST]
        ENGS = [nc.sync, nc.scalar, nc.vector, nc.tensor, nc.gpsimd]

        rot_sem = ctx.enter_context(nc.semaphore("rot_sem"))
        rot_n = [0]

        def barrier(rotate=False):
            for e in ENGS:
                for ch in CHANS:
                    ch.wait(e)

        def collective(name, in_t, out_t):
            s = ctx.enter_context(nc.semaphore("cc_" + name))
            CCs[name] = s
            for ch in CHANS:
                ch.wait(nc.gpsimd)
            nc.gpsimd.collective_compute(
                "AllGather", ALU.bypass,
                replica_groups=[list(range(NCORES))],
                ins=[in_t.ap().opt()], outs=[out_t.ap().opt()],
            ).then_inc(s, 1)
            for e in ENGS:
                e.wait_ge(s, 1)

        def cc_wait(name, eng):
            eng.wait_ge(CCs[name], 1)

        def psum_gate():
            DV.wait(nc.tensor)
            AC.wait(nc.tensor)

        dv = DV.inc
        ac = AC.inc

        def dvf(instr):
            n = DV.inc(instr)
            nc.vector.wait_ge(DV.sem, n)
            return n

        def acf(instr):
            n = AC.inc(instr)
            nc.scalar.wait_ge(AC.sem, n)
            return n

        def emit_rsqrt(dst, srcv, scr, ncols):
            a = scr[:, 0:ncols]
            b = scr[:, ncols:2 * ncols]
            t = scr[:, 2 * ncols:3 * ncols]
            dvf(nc.vector.tensor_scalar(a.bitcast(I32), srcv.bitcast(I32), 1,
                                        None, ALU.logical_shift_right))
            dvf(nc.vector.tensor_scalar(b.bitcast(I32), a.bitcast(I32), -1,
                                        0x5F3759DF, ALU.mult, ALU.add))
            y, yn = b, dst
            for _ in range(4):
                dvf(nc.vector.tensor_copy(a, y))
                dvf(nc.vector.tensor_mul(t, y, a))
                dvf(nc.vector.tensor_mul(t, t, srcv))
                dvf(nc.vector.tensor_scalar(t, t, -0.5, 1.5, ALU.mult, ALU.add))
                dvf(nc.vector.tensor_mul(yn, t, a))
                y, yn = yn, y
            if y is not dst:
                dvf(nc.vector.tensor_copy(dst, y))

        x_sb = ctx.enter_context(nc.sbuf_tensor("x_sb", [128, 8 * D], F32))
        smask = ctx.enter_context(nc.sbuf_tensor("smask", [128, 1024], F32))
        cmask = ctx.enter_context(nc.sbuf_tensor("cmask", [CH, CH], BF16))
        r1hot = ctx.enter_context(nc.sbuf_tensor("r1hot", [KB, 8 * VB], F32))
        onesc = ctx.enter_context(nc.sbuf_tensor("onesc", [128, 1], F32))
        onesb = ctx.enter_context(nc.sbuf_tensor("onesb", [128, 1], BF16))
        onesr = ctx.enter_context(nc.sbuf_tensor("onesr", [1, 128], F32))
        ind_sb = ctx.enter_context(nc.sbuf_tensor("ind_sb", [1, 512], F32))
        ident = ctx.enter_context(nc.sbuf_tensor("ident", [128, 128], BF16))
        rk_sb = ctx.enter_context(nc.sbuf_tensor("rk_sb", [1, 1], I32))
        negb_sb = ctx.enter_context(nc.sbuf_tensor("negb_sb", [128, 2], F32))
        stats = ctx.enter_context(nc.sbuf_tensor("stats", [128, 8], F32))
        stats2 = ctx.enter_context(nc.sbuf_tensor("stats2", [128, 4], F32))
        rmax_sb = ctx.enter_context(nc.sbuf_tensor("rmax_sb", [128, 8], F32))
        dbgst_sb = ctx.enter_context(nc.sbuf_tensor("dbgst_sb", [128, 8], F32))
        eps_sb = ctx.enter_context(nc.sbuf_tensor("eps_sb", [128, 2], F32))
        bwq_sb = ctx.enter_context(nc.sbuf_tensor("bwq_sb", [128, 16 * KB], BF16))
        bwk_sb = ctx.enter_context(nc.sbuf_tensor("bwk_sb", [128, 16 * KB], BF16))
        bwb_sb = ctx.enter_context(nc.sbuf_tensor("bwb_sb", [128, 16 * KB], BF16))
        bwd_sb = ctx.enter_context(nc.sbuf_tensor("bwd_sb", [128, 16 * KB], BF16))
        bwv_sb = ctx.enter_context(nc.sbuf_tensor("bwv_sb", [128, 16 * VB], BF16))
        bwo_sb = ctx.enter_context(nc.sbuf_tensor("bwo_sb", [128, D], BF16))
        bbb_sb = ctx.enter_context(nc.sbuf_tensor("bbb_sb", [KB, 2], F32))
        bbd_sb = ctx.enter_context(nc.sbuf_tensor("bbd_sb", [KB, 2], F32))

        for t_sb, t_in in [(smask, smask_in), (cmask, cmask_in), (r1hot, r1hot_in),
                           (onesc, onesc_in), (onesb, onesb_in), (onesr, onesr_in),
                           (ind_sb, ind_in), (ident, ident_in), (rk_sb, rk_in),
                           (eps_sb, eps_in)]:
            LD.dinc(nc.sync.dma_start(out=t_sb[:], in_=t_in[:, :]))
        for t in range(8):
            LD.dinc(nc.sync.dma_start(out=x_sb[:, t * D:(t + 1) * D],
                                      in_=x_in[128 * t:128 * (t + 1), :]))

        def emit_ln():
            with nc.sbuf_tensor(gs("ln_scb"), [128, D], BF16) as scb, \
                 nc.sbuf_tensor(gs("ln_scr"), [128, D], F32) as scr, \
                 nc.sbuf_tensor(gs("ln_st"), [128, 72], F32) as lst, \
                 nc.sbuf_tensor(gs("ln_xn"), [128, D], BF16) as xn_t:
                barrier()
                for t in range(8):
                    xt = x_sb[:, t * D:(t + 1) * D]
                    dvf(nc.vector.reduce_sum(lst[:, t:t + 1], xt, axis=AX.X))
                    dvf(nc.vector.tensor_copy(scb[:], xt))
                    dvf(nc.vector.tensor_mul(scr[:], xt, scb[:]))
                    dvf(nc.vector.reduce_sum(lst[:, 8 + t:9 + t], scr[:], axis=AX.X))
                dvf(nc.vector.tensor_scalar(lst[:, 16:24], lst[:, 0:8], 1.0 / D,
                                            None, ALU.mult))
                dvf(nc.vector.tensor_copy(lst[:, 48:56], lst[:, 16:24]))
                dvf(nc.vector.tensor_mul(lst[:, 56:64], lst[:, 16:24], lst[:, 48:56]))
                dvf(nc.vector.tensor_scalar(lst[:, 24:32], lst[:, 8:16], 1.0 / D,
                                            None, ALU.mult))
                dvf(nc.vector.tensor_sub(lst[:, 24:32], lst[:, 24:32], lst[:, 56:64]))
                dvf(nc.vector.tensor_scalar(lst[:, 24:32], lst[:, 24:32], 1e-5,
                                            None, ALU.add))
                emit_rsqrt(lst[:, 32:40], lst[:, 24:32], lst[:, 48:72], 8)
                dvf(nc.vector.tensor_mul(lst[:, 40:48], lst[:, 16:24], lst[:, 32:40]))
                dvf(nc.vector.tensor_scalar(lst[:, 40:48], lst[:, 40:48], -1.0,
                                            None, ALU.mult))
                for t in range(8):
                    xt = x_sb[:, t * D:(t + 1) * D]
                    ST.wait(nc.vector)
                    dv(nc.vector.tensor_scalar(scr[:], xt, lst[:, 32 + t:33 + t],
                                               None, ALU.mult))
                    dvf(nc.vector.tensor_scalar(xn_t[:], scr[:], lst[:, 40 + t:41 + t],
                                                None, ALU.add))
                    DV.wait(nc.sync)
                    LD2.wait(nc.sync)
                    ST.dinc(nc.sync.dma_start(
                        out=xn_tok_d[128 * t:128 * (t + 1), :], in_=xn_t[:]))

        def emit_xn_transpose(dst):
            with nc.sbuf_tensor(gs("tx_sb"), [128, TL], BF16,
                                is_transpose_dest=True) as tx:
                LD.wait(nc.scalar)
                for dc in range(16):
                    ST.wait(nc.scalar)
                    tok = LD2.dinc(nc.scalar.dma_start(
                        out=tx[:, 0:TL], in_=xn_tok_d[:, 128 * dc:128 * (dc + 1)],
                        transpose=True))
                    nc.scalar.wait_ge(LD2.sem, tok)
                    ST.dinc(nc.scalar.dma_start(
                        out=dst[128 * dc:128 * (dc + 1), :], in_=tx[:, 0:TL]))

        def emit_wprep(src, dst_T, ncols, nrb=16):
            with nc.sbuf_tensor(gs("wp_f"), [128, ncols], F32) as wf, \
                 nc.sbuf_tensor(gs("wp_b"), [128, ncols], BF16) as wb:
                for rb in range(nrb):
                    DV.wait(nc.sync)
                    ST.wait(nc.sync)
                    tok = LD.dinc(nc.sync.dma_start(
                        out=wf[:], in_=src[128 * rb:128 * (rb + 1), :]))
                    nc.vector.wait_ge(LD.sem, tok)
                    ST.wait(nc.vector)
                    dvf(nc.vector.tensor_copy(wb[:], wf[:]))
                    DV.wait(nc.sync)
                    ST.dinc(nc.sync.dma_start(
                        out=dst_T[128 * rb:128 * (rb + 1), :], in_=wb[:]))

        emit_wprep(wq_s, wqT_d.ap(), 2 * HD)
        emit_wprep(wk_s, wkT_d.ap(), 2 * HD)
        emit_wprep(wv_s, wvT_d.ap(), 2 * HD)
        emit_wprep(wo_s, agwo_in.ap(), D, nrb=2)
        collective("wo", agwo_in, agwo_out)

        def load_bw(dst_sb, src, width):
            with nc.sbuf_tensor(gs("bwf"), [128, 16 * width], F32) as bwf:
                barrier()
                tok = LD.dinc(nc.sync.dma_start(
                    out=bwf[:], in_=src.ap().rearrange("(a p) k -> p a k", p=128)))
                nc.vector.wait_ge(LD.sem, tok)
                dv(nc.vector.tensor_copy(dst_sb[:], bwf[:]))

        load_bw(bwq_sb, bwq, KB)
        load_bw(bwk_sb, bwk, KB)
        load_bw(bwb_sb, bwb, KB)
        load_bw(bwd_sb, bwd, KB)
        load_bw(bwv_sb, bwv, VB)
        with nc.sbuf_tensor("bwof", [128, D], F32) as bwof:
            barrier()
            tok = LD.dinc(nc.sync.dma_start(out=bwof[:], in_=bwo[:, :]))
            nc.vector.wait_ge(LD.sem, tok)
            dv(nc.vector.tensor_copy(bwo_sb[:], bwof[:]))
        LD.dinc(nc.sync.dma_start(out=bbb_sb[:, 0:1], in_=bbb[:, :]))
        LD.dinc(nc.sync.dma_start(out=bbd_sb[:, 0:1], in_=bbd[:, :]))

        emit_ln()
        emit_xn_transpose(agxn_in)
        collective("xn", agxn_in, agxn_out)

        def emit_dequant(packed_t, scales_t, nsc, ag_in_t, out_rows, ocols, dump=False):
            ncc = out_rows // 512
            pk_u = packed_t.ap()
            nsl = (ocols + 511) // 512
            with nc.sbuf_tensor(gs("dq_pk"), [128, ocols], U16) as pk_sb, \
                 nc.sbuf_tensor(gs("dq_m"), [128, ocols], U16) as m_t, \
                 nc.sbuf_tensor(gs("dq_a"), [128, ocols], U16) as a_t, \
                 nc.sbuf_tensor(gs("dq_b"), [128, ocols], U16) as b_t, \
                 nc.sbuf_tensor(gs("dq_k"), [128, ocols], U16) as k_t, \
                 nc.sbuf_tensor(gs("dq_s"), [128, ocols], U16) as s_t, \
                 nc.sbuf_tensor(gs("dq_w"), [128, ocols], BF16) as w_t, \
                 nc.sbuf_tensor(gs("dq_scr"), [1, nsc * ocols], F32) as sc_row, \
                 nc.sbuf_tensor(gs("dq_s128"), [128, ocols], F32) as sc128, \
                 nc.psum_tensor(gs("dq_ps"), [128, ocols], F32) as ps_sc:
                barrier()
                tokg = LDG.dinc(nc.gpsimd.dma_start(
                    out=sc_row[:],
                    in_=scales_t.ap().rearrange("a b -> (a b)").unsqueeze(0)))
                for cc in range(ncc):
                    DV.wait(nc.scalar)
                    tok = LD2.dinc(nc.scalar.dma_start(
                        out=pk_sb[:], in_=pk_u[128 * cc:128 * (cc + 1), :]))
                    nc.tensor.wait_ge(LDG.sem, tokg)
                    LD.wait(nc.tensor)
                    psum_gate()
                    mm = None
                    for g in range(4):
                        for sl in range(nsl):
                            c0 = 512 * sl
                            c1 = min(ocols, c0 + 512)
                            mm = nc.tensor.matmul(
                                ps_sc[:, c0:c1],
                                ind_sb[0:1, 128 * g:128 * (g + 1)],
                                sc_row[0:1, (4 * cc + g) * ocols + c0:
                                       (4 * cc + g) * ocols + c1],
                                start=(g == 0), stop=(g == 3))
                    ptok = PE.inc(mm)
                    nc.vector.wait_ge(PE.sem, ptok)
                    dv(nc.vector.tensor_copy(sc128[:], ps_sc[:]))
                    nc.vector.wait_ge(LD2.sem, tok)
                    for j in range(4):
                        dvf(nc.vector.tensor_scalar(
                            m_t[:], pk_sb[:], 4 * j, 7,
                            ALU.logical_shift_right, ALU.bitwise_and))
                        dvf(nc.vector.tensor_scalar(a_t[:], m_t[:], 64, 0x3F00,
                                                    ALU.mult, ALU.add))
                        dvf(nc.vector.tensor_scalar(b_t[:], m_t[:], 0x3F00,
                                                    None, ALU.mult))
                        dvf(nc.vector.tensor_tensor(a_t[:], a_t[:], b_t[:], ALU.min))
                        dvf(nc.vector.tensor_scalar(
                            s_t[:], pk_sb[:], 4 * j + 3, 1,
                            ALU.logical_shift_right, ALU.bitwise_and))
                        dvf(nc.vector.scalar_tensor_tensor(
                            s_t[:], s_t[:], 0x8000, a_t[:], ALU.mult, ALU.add))
                        if dump and DEBUG and cc == 0 and j == 0:
                            DV.wait(nc.sync)
                            ST.dinc(nc.sync.dma_start(out=dbg_scr[:, :],
                                                      in_=sc_row[:]))
                            ST.dinc(nc.sync.dma_start(out=dbg_s128[:, :],
                                                      in_=sc128[:]))
                            ST.dinc(nc.sync.dma_start(out=dbg_m[:, :], in_=m_t[:]))
                            ST.dinc(nc.sync.dma_start(out=dbg_a[:, :], in_=a_t[:]))
                            ST.dinc(nc.sync.dma_start(out=dbg_pat[:, :], in_=s_t[:]))
                            ST.dinc(nc.sync.dma_start(out=dbg_pk[:, :], in_=pk_sb[:]))
                        ST.wait(nc.vector)
                        dvf(nc.vector.tensor_mul(w_t[:], s_t.bitcast(BF16)[:],
                                                 sc128[:]))
                        if dump and DEBUG and cc == 0 and j == 0:
                            DV.wait(nc.sync)
                            ST.dinc(nc.sync.dma_start(out=dbg_wt[:, :], in_=w_t[:]))
                        DV.wait(nc.sync)
                        vv = ag_in_t.ap().rearrange("(a p b) o -> a b p o",
                                                    p=128, b=4)
                        ST.dinc(nc.sync.dma_start(out=vv[cc, j], in_=w_t[:]))

        if STAGE >= 2:
            emit_dequant(gup_s, gup_sc, 16, aggu_in, D, 1408, dump=True)
            emit_dequant(dwn_s, dwn_sc, 44, agdw_in, I_FF, 256)

        cc_wait("xn", nc.sync)
        with nc.sbuf_tensor("wqkv", [128, 16 * 768], BF16) as wqkv, \
             nc.sbuf_tensor("xf0", [128, TL], BF16) as xf0, \
             nc.sbuf_tensor("xf1", [128, TL], BF16) as xf1, \
             nc.sbuf_tensor("qs0", [128, TL], BF16) as qs0, \
             nc.sbuf_tensor("qs1", [128, TL], BF16) as qs1, \
             nc.sbuf_tensor("vstage", [128, 2048], BF16) as vstage:
            barrier()
            for (wsrc, off) in [(wqT_d, 0), (wkT_d, 256), (wvT_d, 512)]:
                ST.wait(nc.sync)
                LD.dinc(nc.sync.dma_start(
                    out=wqkv.ap().rearrange("p (a c) -> p a c", c=768)[:, :, off:off + 256],
                    in_=wsrc.ap().rearrange("(a p) c -> p a c", p=128)))
            dv(nc.vector.memset(rmax_sb[:], 0.0))
            xbufs = [xf0, xf1]
            buf_tok = [0, 0]
            evac_dv = 0
            evac_ac = 0
            with nc.psum_tensor("ps_q0", [128, TL], F32) as ps_q0, \
                 nc.psum_tensor("ps_q1", [128, TL], F32) as ps_q1, \
                 nc.psum_tensor("ps_k0", [128, TL], F32) as ps_k0, \
                 nc.psum_tensor("ps_k1", [128, TL], F32) as ps_k1:
                for r in range(NCORES):
                    for dc in range(16):
                        b = dc % 2
                        if buf_tok[b] > 0:
                            nc.sync.wait_ge(PE.sem, buf_tok[b])
                        ltok = LD.dinc(nc.sync.dma_start(
                            out=xbufs[b][:],
                            in_=agxn_out[D * r + 128 * dc:D * r + 128 * (dc + 1), :]))
                        nc.tensor.wait_ge(LD.sem, ltok)
                        if dc == 0:
                            nc.tensor.wait_ge(DV.sem, evac_dv if r else DV.n)
                            nc.tensor.wait_ge(AC.sem, evac_ac if r else AC.n)
                        mm = None
                        for (ps, woff) in [(ps_q0, 0), (ps_q1, 128),
                                           (ps_k0, 256), (ps_k1, 384)]:
                            for th in range(2):
                                mm = nc.tensor.matmul(
                                    ps[:, 512 * th:512 * (th + 1)],
                                    wqkv[:, 768 * dc + woff:768 * dc + woff + 128],
                                    xbufs[b][:, 512 * th:512 * (th + 1)],
                                    start=(dc == 0), stop=(dc == 15))
                        buf_tok[b] = PE.inc(mm)
                    nc.vector.wait_ge(PE.sem, buf_tok[1])
                    nc.scalar.wait_ge(PE.sem, buf_tok[1])
                    for (o2, ps) in [(0, ps_q0), (1, ps_q1)]:
                        qdst = [qs0, qs1][o2]
                        ST.wait(nc.vector)
                        dvf(nc.vector.tensor_scalar(qdst[:], ps[:], QK_SCALE, None,
                                                    ALU.mult))
                        DV.wait(nc.sync)
                        ST.dinc(nc.sync.dma_start(
                            out=q_d[128 * o2:128 * (o2 + 1), TL * r:TL * (r + 1)],
                            in_=qdst[:]))
                    for (o2, ps) in [(0, ps_k0), (1, ps_k1)]:
                        kdst = [qs0, qs1][o2]
                        DV.wait(nc.scalar)
                        ST.wait(nc.scalar)
                        acf(nc.scalar.activation(kdst[:], ps[:], AF.Copy))
                        LD2.wait(nc.scalar)
                        ST.dinc(nc.scalar.dma_start(
                            out=k_d[128 * o2:128 * (o2 + 1), TL * r:TL * (r + 1)],
                            in_=kdst[:]))
                    evac_dv = DV.n
                    evac_ac = AC.n
            with nc.sbuf_tensor("nb_scr", [128, 1024], BF16) as nb_scr, \
                 nc.sbuf_tensor("nb_cp", [128, 1024], BF16) as nb_cp, \
                 nc.sbuf_tensor("nb_tile", [128, 1024], BF16) as nb_tile, \
                 nc.sbuf_tensor("nb_st", [128, 64], F32) as nbst, \
                 nc.psum_tensor("ps_nrm", [1, 1024], F32) as ps_nrm, \
                 nc.psum_tensor("ps_nb", [128, 4], F32) as ps_nb:
                barrier()
                dvf(nc.vector.memset(nbst[:, 0:64], 0.0))
                psum_gate()
                ST.wait(nc.sync)
                LD2.wait(nc.sync)
                nb_cons = 0
                for (vi, src_d) in [(0, q_d), (1, q_d), (2, k_d), (3, k_d)]:
                    o2 = vi % 2
                    for rr in range(8):
                        if nb_cons > 0:
                            nc.sync.wait_ge(DV.sem, nb_cons)
                        ltok = LD.dinc(nc.sync.dma_start(
                            out=nb_tile[:],
                            in_=src_d[128 * o2:128 * (o2 + 1),
                                      1024 * rr:1024 * (rr + 1)]))
                        nc.vector.wait_ge(LD.sem, ltok)
                        PE.wait(nc.vector)
                        dvf(nc.vector.tensor_copy(nb_cp[:], nb_tile[:]))
                        nb_cons = dvf(nc.vector.tensor_mul(nb_scr[:], nb_tile[:],
                                                           nb_cp[:]))
                        nc.tensor.wait_ge(DV.sem, DV.n)
                        mm = None
                        for sl in range(2):
                            mm = nc.tensor.matmul(
                                ps_nrm[0:1, 512 * sl:512 * (sl + 1)], onesb[:, 0:1],
                                nb_scr[:, 512 * sl:512 * (sl + 1)],
                                start=True, stop=True)
                        ptok = PE.inc(mm)
                        nc.vector.wait_ge(PE.sem, ptok)
                        dvf(nc.vector.reduce_max(nbst[0:1, 8 * vi + rr:8 * vi + rr + 1],
                                                 ps_nrm[0:1, :], axis=AX.X))
                for vi in range(4):
                    dvf(nc.vector.reduce_max(nbst[0:1, 32 + vi:33 + vi],
                                             nbst[0:1, 8 * vi:8 * vi + 8], axis=AX.X))
                dvf(nc.vector.tensor_mul(nbst[0:1, 36:38], nbst[0:1, 32:34],
                                         nbst[0:1, 34:36]))
                emit_rsqrt(nbst[0:1, 38:40], nbst[0:1, 36:38], nbst[0:1, 40:46], 2)
                dvf(nc.vector.tensor_mul(nbst[0:1, 38:40], nbst[0:1, 38:40],
                                         nbst[0:1, 36:38]))
                dvf(nc.vector.tensor_scalar(nbst[0:1, 38:40], nbst[0:1, 38:40], -1.01,
                                            None, ALU.mult))
                if DEBUG:
                    DV.wait(nc.sync)
                    ST.dinc(nc.sync.dma_start(out=dbg_nrm[:, :], in_=nbst[0:1, 0:16]))
                DV.wait(nc.tensor)
                ptok = PE.inc(nc.tensor.matmul(ps_nb[:, 0:2], onesr[0:1, :],
                                               nbst[0:1, 38:40], start=True, stop=True))
                nc.vector.wait_ge(PE.sem, ptok)
                dvf(nc.vector.tensor_copy(negb_sb[:], ps_nb[:, 0:2]))
            with nc.psum_tensor("ps_v", [128, 4096], F32) as ps_v:
                barrier()
                buf_tok = [0, 0]
                vtok_dv = DV.n
                for r in range(NCORES):
                    for dc in range(16):
                        b = dc % 2
                        nc.sync.wait_ge(PE.sem, buf_tok[b])
                        ltok = LD.dinc(nc.sync.dma_start(
                            out=xbufs[b][:],
                            in_=agxn_out[D * r + 128 * dc:D * r + 128 * (dc + 1), :]))
                        nc.tensor.wait_ge(LD.sem, ltok)
                        if dc == 0:
                            nc.tensor.wait_ge(DV.sem, vtok_dv)
                            if r == 0:
                                AC.wait(nc.tensor)
                        mm = None
                        for sb8 in range(8):
                            mm = nc.tensor.matmul(
                                ps_v[:, 512 * sb8:512 * sb8 + 256],
                                xbufs[b][:, 128 * sb8:128 * (sb8 + 1)],
                                wqkv[:, 768 * dc + 512:768 * dc + 768],
                                start=(dc == 0), stop=(dc == 15))
                        buf_tok[b] = PE.inc(mm)
                    nc.vector.wait_ge(PE.sem, buf_tok[1])
                    ST.wait(nc.vector)
                    dv(nc.vector.tensor_copy(
                        vstage.ap().rearrange("p (s v) -> p s v", v=256),
                        ps_v.ap().rearrange("p (s v) -> p s v", v=512)[:, :, 0:256]))
                    vtok_dv = DV.n
                    DV.wait(nc.sync)
                    ST.dinc(nc.sync.dma_start(
                        out=v_d.ap().rearrange("(r s p) v -> r p s v", p=128, s=8)[r],
                        in_=vstage.ap().rearrange("p (s v) -> p s v", v=256)))

        with nc.sbuf_tensor("kf_sb", [128, T], BF16) as kf_sb, \
             nc.sbuf_tensor("vt_sb", [128, 64 * 128], BF16) as vt_sb, \
             nc.sbuf_tensor("qt0", [128, 512], BF16) as qt0, \
             nc.sbuf_tensor("qt1", [128, 512], BF16) as qt1, \
             nc.sbuf_tensor("pt_sb", [128, 2048], BF16) as pt_sb, \
             nc.sbuf_tensor("sum_sb", [1, 512], F32) as sum_sb, \
             nc.sbuf_tensor("rec_sb", [1, 512], F32) as rec_sb, \
             nc.sbuf_tensor("rec128", [128, 512], F32) as rec128, \
             nc.sbuf_tensor("zstage", [128, 512], BF16) as zstage, \
             nc.psum_tensor("ps_s0", [128, 1024], F32) as ps_s0, \
             nc.psum_tensor("ps_s1", [128, 1024], F32) as ps_s1, \
             nc.psum_tensor("ps_sum", [1, 512], F32) as ps_sum, \
             nc.psum_tensor("ps_pv", [128, 512], F32) as ps_pv, \
             nc.psum_tensor("ps_rec", [128, 512], F32) as ps_rec:
            barrier()
            qbufs = [qt0, qt1]
            sbufs = [ps_s0, ps_s1]
            for h in range(HLOC):
                if h > 0:
                    barrier()
                ktok = LD.dinc(nc.sync.dma_start(
                    out=kf_sb[:], in_=k_d[128 * h:128 * (h + 1), :]))
                vtok = LD.dinc(nc.sync.dma_start(
                    out=vt_sb.ap().rearrange("p (s v) -> p s v", v=128),
                    in_=v_d.ap().rearrange("(s p) (g v) -> g p s v", p=128, g=2)[h]))
                nc.tensor.wait_ge(LD.sem, vtok)
                qld_tok = [0, 0]
                pv_done_tok = 0
                for tb in range(16):
                    bq = tb % 2
                    if qld_tok[bq] > 0:
                        nc.sync.wait_ge(PE.sem, qld_tok[bq])
                    qtok = LD.dinc(nc.sync.dma_start(
                        out=qbufs[bq][:],
                        in_=q_d[128 * h:128 * (h + 1), 512 * tb:512 * (tb + 1)]))
                    nc.tensor.wait_ge(LD.sem, qtok)
                    DV.wait(nc.tensor)
                    if h == 0 and tb == 0:
                        psum_gate()
                    ngrp = 2 * (tb + 1)
                    pe_tok = 0
                    for g in range(ngrp):
                        pg = sbufs[g % 2]
                        slot = 1024 * (g % 2)
                        AC.wait(nc.tensor)
                        mm = None
                        for i in range(2):
                            sc = 2 * g + i
                            mm = nc.tensor.matmul(
                                pg[:, 512 * i:512 * (i + 1)],
                                kf_sb[:, 128 * sc:128 * (sc + 1)],
                                qbufs[bq][:], start=True, stop=True)
                        smm = PE.inc(mm)
                        diag = (g >= ngrp - 2)
                        if diag:
                            nc.vector.wait_ge(PE.sem, smm)
                            for i in range(2):
                                sc = 2 * g + i
                                off = 128 * (sc - 4 * tb)
                                dvf(nc.vector.tensor_tensor(
                                    pg[:, 512 * i:512 * (i + 1)],
                                    pg[:, 512 * i:512 * (i + 1)],
                                    smask[:, 512 - off:1024 - off], ALU.add))
                            nc.scalar.wait_ge(DV.sem, DV.n)
                        nc.scalar.wait_ge(PE.sem, PE.n)
                        acf(nc.scalar.activation(
                            pt_sb[:, slot:slot + 1024], pg[:],
                            AF.Exp, bias=negb_sb[:, h:h + 1], scale=1.0))
                        AC.wait(nc.tensor)
                        for i in range(2):
                            sc = 2 * g + i
                            nc.tensor.matmul(
                                ps_sum[0:1, :], onesb[:, 0:1],
                                pt_sb[:, slot + 512 * i:slot + 512 * (i + 1)],
                                start=(sc == 0), stop=(sc == ngrp * 2 - 1))
                            mm = nc.tensor.matmul(
                                ps_pv[:, :], vt_sb[:, 128 * sc:128 * (sc + 1)],
                                pt_sb[:, slot + 512 * i:slot + 512 * (i + 1)],
                                start=(sc == 0), stop=(sc == ngrp * 2 - 1))
                        pe_tok = PE.inc(mm)
                    pv_done_tok = pe_tok
                    qld_tok[bq] = pe_tok
                    nc.vector.wait_ge(PE.sem, pe_tok)
                    ST.wait(nc.vector)
                    dvf(nc.vector.tensor_copy(sum_sb[:], ps_sum[0:1, :]))
                    dvf(nc.vector.reciprocal(rec_sb[:], sum_sb[:]))
                    DV.wait(nc.tensor)
                    rb_tok = PE.inc(nc.tensor.matmul(ps_rec[:], onesr[0:1, :],
                                                     rec_sb[:], start=True, stop=True))
                    nc.vector.wait_ge(PE.sem, rb_tok)
                    dvf(nc.vector.tensor_copy(rec128[:], ps_rec[:]))
                    dvf(nc.vector.tensor_mul(zstage[:], ps_pv[:], rec128[:]))
                    DV.wait(nc.sync)
                    ST.dinc(nc.sync.dma_start(
                        out=agz_in[128 * h:128 * (h + 1), 512 * tb:512 * (tb + 1)],
                        in_=zstage[:]))
        collective("z", agz_in, agz_out)

        cc_wait("z", nc.sync)
        zc_sem = ctx.enter_context(nc.semaphore("zc_sem"))
        reg = nc.sync.alloc_register("rkreg")
        nc.sync.load(reg, rk_sb[0:1, 0:1])
        for t in range(NCORES):
            with nc.sync.If_eq(reg, t):
                nc.sync.dma_start(
                    out=zloc_d[:, :], in_=agz_out[:, TL * t:TL * (t + 1)]
                ).then_inc(zc_sem, 16)
        nc.sync.free_register(reg)
        nc.sync.wait_ge(zc_sem, 16)

        cc_wait("wo", nc.sync)
        with nc.sbuf_tensor("zt0", [128, 256], BF16) as zt0, \
             nc.sbuf_tensor("zt1", [128, 256], BF16) as zt1, \
             nc.sbuf_tensor("wo0", [128, D], BF16) as wo0, \
             nc.sbuf_tensor("wo1", [128, D], BF16) as wo1, \
             nc.psum_tensor("ps_wo", [128, 4096], F32) as ps_wo:
            barrier()
            zb = [zt0, zt1]
            wb_ = [wo0, wo1]
            res_tok = 0
            for tp in range(4):
                for dc in range(16):
                    b = dc % 2
                    nc.sync.wait_ge(PE.sem, PE.n)
                    ztok = LD.dinc(nc.sync.dma_start(
                        out=zb[b][:],
                        in_=zloc_d[128 * dc:128 * (dc + 1), 256 * tp:256 * (tp + 1)]))
                    wtok = LD.dinc(nc.sync.dma_start(
                        out=wb_[b][:], in_=agwo_out[128 * dc:128 * (dc + 1), :]))
                    nc.tensor.wait_ge(LD.sem, wtok)
                    if dc == 0:
                        nc.tensor.wait_ge(DV.sem, res_tok if tp else DV.n)
                        if tp == 0:
                            AC.wait(nc.tensor)
                    mm = None
                    for tt in range(2):
                        for do4 in range(4):
                            mm = nc.tensor.matmul(
                                ps_wo[:, 2048 * tt + 512 * do4:
                                      2048 * tt + 512 * (do4 + 1)],
                                zb[b][:, 128 * tt:128 * (tt + 1)],
                                wb_[b][:, 512 * do4:512 * (do4 + 1)],
                                start=(dc == 0), stop=(dc == 15))
                    PE.inc(mm)
                nc.vector.wait_ge(PE.sem, PE.n)
                for tt in range(2):
                    t8 = 2 * tp + tt
                    xt = x_sb[:, t8 * D:(t8 + 1) * D]
                    dv(nc.vector.tensor_add(xt, xt,
                                            ps_wo[:, 2048 * tt:2048 * (tt + 1)]))
                res_tok = DV.n

        if STAGE >= 2:
            collective("gu", aggu_in, aggu_out)
            collective("dw", agdw_in, agdw_out)

        def emit_ffn():
            emit_ln()
            emit_xn_transpose(xnf_d)
            cc_wait("gu", nc.sync)
            cc_wait("dw", nc.sync)
            with nc.sbuf_tensor(gs("xnf_sb"), [128, 16 * TL], BF16) as xnf_sb, \
                 nc.sbuf_tensor(gs("wg0"), [128, 2048], BF16) as wg0, \
                 nc.sbuf_tensor(gs("wg1"), [128, 2048], BF16) as wg1, \
                 nc.sbuf_tensor(gs("wu0"), [128, 2048], BF16) as wu0, \
                 nc.sbuf_tensor(gs("wu1"), [128, 2048], BF16) as wu1, \
                 nc.sbuf_tensor(gs("silu_sb"), [128, TL], BF16) as silu_sb, \
                 nc.sbuf_tensor(gs("hh_sb"), [128, TL], BF16) as hh_sb:
                barrier()
                for dc in range(16):
                    LD.dinc(nc.sync.dma_start(
                        out=xnf_sb[:, TL * dc:TL * (dc + 1)],
                        in_=xnf_d[128 * dc:128 * (dc + 1), :]))
                wgb = [wg0, wg1]
                wub = [wu0, wu1]
                wg_tok = [0, 0]
                ev_pair = [[0, 0], [0, 0]]
                with nc.psum_tensor(gs("ps_g0"), [128, TL], F32) as ps_g0, \
                     nc.psum_tensor(gs("ps_u0"), [128, TL], F32) as ps_u0, \
                     nc.psum_tensor(gs("ps_g1"), [128, TL], F32) as ps_g1, \
                     nc.psum_tensor(gs("ps_u1"), [128, TL], F32) as ps_u1:
                    psg = [ps_g0, ps_g1]
                    psu = [ps_u0, ps_u1]
                    for ocp in range(44):
                        b = ocp % 2
                        if wg_tok[b] > 0:
                            nc.sync.wait_ge(PE.sem, wg_tok[b])
                        for (wbuf, oc) in [(wgb[b], ocp), (wub[b], ocp + 44)]:
                            rnk = oc // 11
                            o0 = 128 * (oc % 11)
                            LD.dinc(nc.sync.dma_start(
                                out=wbuf.ap().rearrange("p (a c) -> p a c", c=128),
                                in_=aggu_out.ap().rearrange(
                                    "(r a p) o -> r p a o", p=128, a=16)
                                [rnk, :, :, o0:o0 + 128]))
                        nc.tensor.wait_ge(LD.sem, LD.n)
                        if ocp < 2:
                            psum_gate()
                        else:
                            nc.tensor.wait_ge(AC.sem, ev_pair[b][0])
                            nc.tensor.wait_ge(DV.sem, ev_pair[b][1])
                        mm = None
                        for hc in range(16):
                            for th in range(2):
                                nc.tensor.matmul(
                                    psg[b][:, 512 * th:512 * (th + 1)],
                                    wgb[b][:, 128 * hc:128 * (hc + 1)],
                                    xnf_sb[:, TL * hc + 512 * th:
                                           TL * hc + 512 * (th + 1)],
                                    start=(hc == 0), stop=(hc == 15))
                                mm = nc.tensor.matmul(
                                    psu[b][:, 512 * th:512 * (th + 1)],
                                    wub[b][:, 128 * hc:128 * (hc + 1)],
                                    xnf_sb[:, TL * hc + 512 * th:
                                           TL * hc + 512 * (th + 1)],
                                    start=(hc == 0), stop=(hc == 15))
                        wg_tok[b] = PE.inc(mm)
                        nc.scalar.wait_ge(PE.sem, wg_tok[b])
                        ST.wait(nc.scalar)
                        a_tok = ac(nc.scalar.activation(silu_sb[:], psg[b][:],
                                                        AF.Silu))
                        nc.vector.wait_ge(AC.sem, a_tok)
                        ST.wait(nc.vector)
                        d_tok = dv(nc.vector.tensor_mul(hh_sb[:], silu_sb[:],
                                                        psu[b][:]))
                        ev_pair[b] = [a_tok, d_tok]
                        DV.wait(nc.sync)
                        ST.dinc(nc.sync.dma_start(
                            out=h_d[128 * ocp:128 * (ocp + 1), :], in_=hh_sb[:]))
            with nc.sbuf_tensor(gs("ht0"), [128, TL], BF16) as ht0, \
                 nc.sbuf_tensor(gs("ht1"), [128, TL], BF16) as ht1, \
                 nc.sbuf_tensor(gs("wd0"), [128, 512], BF16) as wd0, \
                 nc.sbuf_tensor(gs("wd1"), [128, 512], BF16) as wd1, \
                 nc.sbuf_tensor(gs("y_sb"), [128, 8 * D], F32) as y_sb, \
                 nc.sbuf_tensor(gs("y2_sb"), [128, D], F32) as y2_sb, \
                 nc.psum_tensor(gs("ps_y"), [128, 4096], F32) as ps_y:
                barrier()
                htb = [ht0, ht1]
                wdb = [wd0, wd1]
                ht_tok = [0, 0]
                yev_tok = 0
                emit_ffn.first = getattr(emit_ffn, "first", True)
                for hoq in range(4):
                    for ic in range(44):
                        b = ic % 2
                        if ht_tok[b] > 0:
                            nc.sync.wait_ge(PE.sem, ht_tok[b])
                        ST.wait(nc.sync) if (hoq == 0 and ic == 0) else None
                        LD.dinc(nc.sync.dma_start(
                            out=htb[b][:], in_=h_d[128 * ic:128 * (ic + 1), :]))
                        for half in range(2):
                            rnk = 2 * hoq + half
                            LD.dinc(nc.sync.dma_start(
                                out=wdb[b][:, 256 * half:256 * (half + 1)],
                                in_=agdw_out[I_FF * rnk + 128 * ic:
                                             I_FF * rnk + 128 * (ic + 1), :]))
                        nc.tensor.wait_ge(LD.sem, LD.n)
                        if ic == 0:
                            psum_gate()
                        mm = None
                        for t8 in range(8):
                            mm = nc.tensor.matmul(
                                ps_y[:, 512 * t8:512 * (t8 + 1)],
                                htb[b][:, 128 * t8:128 * (t8 + 1)],
                                wdb[b][:], start=(ic == 0), stop=(ic == 43))
                        ht_tok[b] = PE.inc(mm)
                    nc.vector.wait_ge(PE.sem, ht_tok[43 % 2])
                    for t8 in range(8):
                        dv(nc.vector.tensor_copy(
                            y_sb[:, t8 * D + 512 * hoq:t8 * D + 512 * (hoq + 1)],
                            ps_y[:, 512 * t8:512 * (t8 + 1)]))
                    yev_tok = DV.n
                if DEBUG and emit_ffn.first:
                    emit_ffn.first = False
                    DV.wait(nc.sync)
                    ST.dinc(nc.sync.dma_start(out=dbg_y[:, :],
                                              in_=y_sb[:, 0:D]))
                barrier()
                with nc.sbuf_tensor(gs("rms_b"), [128, D], BF16) as rmsb, \
                     nc.sbuf_tensor(gs("rms_st"), [128, 48], F32) as rst:
                    AC.wait(nc.vector)
                    for t8 in range(8):
                        yt = y_sb[:, t8 * D:(t8 + 1) * D]
                        dvf(nc.vector.tensor_copy(rmsb[:], yt))
                        dvf(nc.vector.tensor_mul(y2_sb[:], yt, rmsb[:]))
                        dvf(nc.vector.reduce_sum(rst[:, t8:t8 + 1], y2_sb[:],
                                                 axis=AX.X))
                    dvf(nc.vector.tensor_scalar(rst[:, 8:16], rst[:, 0:8], 1.0 / D,
                                                1e-6, ALU.mult, ALU.add))
                    emit_rsqrt(rst[:, 16:24], rst[:, 8:16], rst[:, 24:48], 8)
                    if DEBUG and getattr(emit_ffn, "first2", True):
                        emit_ffn.first2 = False
                        DV.wait(nc.sync)
                        ST.dinc(nc.sync.dma_start(out=dbg_rst[:, :], in_=rst[:]))
                    for t8 in range(8):
                        yt = y_sb[:, t8 * D:(t8 + 1) * D]
                        xt = x_sb[:, t8 * D:(t8 + 1) * D]
                        dv(nc.vector.tensor_scalar(y2_sb[:], yt,
                                                   rst[:, 16 + t8:17 + t8], None,
                                                   ALU.mult))
                        dvf(nc.vector.tensor_add(xt, xt, y2_sb[:]))

        if STAGE >= 2:
            emit_ffn()
            if DEBUG:
                barrier()
                for t in range(8):
                    ST.dinc(nc.sync.dma_start(
                        out=dbg_x3[128 * t:128 * (t + 1), :],
                        in_=x_sb[:, t * D:(t + 1) * D]))

        if STAGE >= 3:
            emit_ln()
            emit_xn_transpose(xnf_d)
            bctx = ExitStack()
            bxnf = bctx.enter_context(nc.sbuf_tensor("bxnf", [128, 16 * TL], BF16))
            bq_f = bctx.enter_context(nc.sbuf_tensor("bq_f", [KB, TL], F32))
            bk_f = bctx.enter_context(nc.sbuf_tensor("bk_f", [KB, TL], F32))
            blb_f = bctx.enter_context(nc.sbuf_tensor("blb_f", [KB, TL], F32))
            bdl_f = bctx.enter_context(nc.sbuf_tensor("bdl_f", [KB, TL], F32))
            bt1 = bctx.enter_context(nc.sbuf_tensor("bt1", [KB, TL], F32))
            bt2 = bctx.enter_context(nc.sbuf_tensor("bt2", [KB, TL], F32))
            bL = bctx.enter_context(nc.sbuf_tensor("bL", [KB, TL], F32))
            beL = bctx.enter_context(nc.sbuf_tensor("beL", [KB, TL], F32))
            bqt = bctx.enter_context(nc.sbuf_tensor("bqt", [KB, TL], BF16))
            bat = bctx.enter_context(nc.sbuf_tensor("bat", [KB, TL], BF16))
            bzero = bctx.enter_context(nc.sbuf_tensor("bzero", [KB, TL], F32))
            vtokb = bctx.enter_context(nc.sbuf_tensor("vtokb", [CH, NCH * VB], BF16))
            attok = bctx.enter_context(nc.sbuf_tensor("attok", [CH, NCH * KB], BF16))
            atm = bctx.enter_context(nc.sbuf_tensor("atm", [CH, CH], BF16))
            S_sb = bctx.enter_context(nc.sbuf_tensor("S_sb", [KB, VB], F32))
            S_bf = bctx.enter_context(nc.sbuf_tensor("S_bf", [KB, VB], BF16))
            Stmp = bctx.enter_context(nc.sbuf_tensor("Stmp", [KB, VB], F32))
            dpr = bctx.enter_context(nc.sbuf_tensor("dpr", [KB, NCH + 2], F32))
            loff = bctx.enter_context(nc.sbuf_tensor("loff", [KB, NCH + 1], F32))
            o_f32 = bctx.enter_context(nc.sbuf_tensor("o_f32", [128, TL], F32))
            o_bf = bctx.enter_context(nc.sbuf_tensor("o_bf", [128, TL], BF16))
            u_all = bctx.enter_context(nc.sbuf_tensor("u_all", [KB, 8 * VB], F32))
            d_all = bctx.enter_context(nc.sbuf_tensor("d_all", [KB, 8], F32))
            pref = bctx.enter_context(nc.sbuf_tensor("pref", [KB, 9 * VB], F32))
            smy0 = bctx.enter_context(nc.sbuf_tensor("smy0", [KB, VB], F32))
            smy1 = bctx.enter_context(nc.sbuf_tensor("smy1", [KB, VB], F32))
            with bctx:
                barrier()
                for dc in range(16):
                    LD.dinc(nc.sync.dma_start(
                        out=bxnf[:, TL * dc:TL * (dc + 1)],
                        in_=xnf_d[128 * dc:128 * (dc + 1), :]))
                nc.tensor.wait_ge(LD.sem, LD.n)
                with nc.psum_tensor("bps0", [KB, TL], F32) as bps0, \
                     nc.psum_tensor("bps1", [KB, TL], F32) as bps1, \
                     nc.psum_tensor("bps2", [KB, TL], F32) as bps2, \
                     nc.psum_tensor("bps3", [KB, TL], F32) as bps3:
                    psum_gate()
                    mm = None
                    for (ps, wsb) in [(bps0, bwq_sb), (bps1, bwk_sb),
                                      (bps2, bwb_sb), (bps3, bwd_sb)]:
                        for dc in range(16):
                            for th in range(2):
                                mm = nc.tensor.matmul(
                                    ps[:, 512 * th:512 * (th + 1)],
                                    wsb[:, KB * dc:KB * (dc + 1)],
                                    bxnf[:, TL * dc + 512 * th:
                                         TL * dc + 512 * (th + 1)],
                                    start=(dc == 0), stop=(dc == 15))
                    ptok = PE.inc(mm)
                    nc.vector.wait_ge(PE.sem, ptok)
                    dv(nc.vector.tensor_copy(bq_f[:], bps0[:]))
                    dv(nc.vector.tensor_copy(bk_f[:], bps1[:]))
                    dv(nc.vector.tensor_scalar(blb_f[:], bps2[:], bbb_sb[:, 0:1],
                                               None, ALU.add))
                    dv(nc.vector.tensor_scalar(bdl_f[:], bps3[:], bbd_sb[:, 0:1],
                                               None, ALU.add))
                with nc.psum_tensor("bpv", [CH, 2048], F32) as bpv:
                    vb_dv = DV.n
                    for cq in range(8):
                        nc.tensor.wait_ge(DV.sem, vb_dv)
                        mm = None
                        for ci in range(4):
                            c = 4 * cq + ci
                            for dc in range(16):
                                mm = nc.tensor.matmul(
                                    bpv[:, 512 * ci:512 * ci + VB],
                                    bxnf[:, TL * dc + CH * c:TL * dc + CH * (c + 1)],
                                    bwv_sb[:, VB * dc:VB * (dc + 1)],
                                    start=(dc == 0), stop=(dc == 15))
                        ptok = PE.inc(mm)
                        nc.vector.wait_ge(PE.sem, ptok)
                        dv(nc.vector.tensor_copy(
                            vtokb.ap().rearrange("p (s v) -> p s v", v=VB)
                            [:, 4 * cq:4 * (cq + 1), :],
                            bpv.ap().rearrange("p (s v) -> p s v", v=512)[:, :, 0:VB]))
                        vb_dv = DV.n
                DV.wait(nc.scalar)
                acf(nc.scalar.activation(bt1[:], blb_f[:], AF.Abs))
                acf(nc.scalar.activation(bt2[:], bt1[:], AF.Exp, bias=0.0, scale=-1.0))
                acf(nc.scalar.activation(bt1[:], bt2[:], AF.Ln, bias=1.0, scale=1.0))
                AC.wait(nc.vector)
                dvf(nc.vector.tensor_scalar(bt2[:], blb_f[:], 0.0, None, ALU.min))
                dvf(nc.vector.tensor_sub(blb_f[:], bt2[:], bt1[:]))
                DV.wait(nc.scalar)
                acf(nc.scalar.activation(bt1[:], bdl_f[:], AF.Abs))
                acf(nc.scalar.activation(bt2[:], bt1[:], AF.Exp, bias=0.0, scale=-1.0))
                acf(nc.scalar.activation(bt1[:], bt2[:], AF.Ln, bias=1.0, scale=1.0))
                AC.wait(nc.vector)
                dvf(nc.vector.tensor_scalar(bt2[:], bdl_f[:], 0.0, None, ALU.max))
                dvf(nc.vector.tensor_add(bdl_f[:], bt2[:], bt1[:]))
                dvf(nc.vector.memset(bzero[:], 0.0))
                for c in range(NCH):
                    dvf(nc.vector.tensor_tensor_scan(
                        bL[:, CH * c:CH * (c + 1)], blb_f[:, CH * c:CH * (c + 1)],
                        bzero[:, CH * c:CH * (c + 1)], 0.0, ALU.add, ALU.add))
                DV.wait(nc.scalar)
                acf(nc.scalar.activation(beL[:], bL[:], AF.Exp, bias=0.0, scale=1.0))
                acf(nc.scalar.activation(bt1[:], bL[:], AF.Exp, bias=0.0, scale=-1.0))
                AC.wait(nc.vector)
                dvf(nc.vector.tensor_mul(bqt[:], bq_f[:], beL[:]))
                dvf(nc.vector.tensor_mul(bt2[:], bk_f[:], bdl_f[:]))
                dvf(nc.vector.tensor_mul(bat[:], bt2[:], bt1[:]))
                with nc.psum_tensor("bpt", [CH, KB], BF16) as bpt, \
                     nc.psum_tensor("bpu", [KB, VB], F32) as bpu, \
                     nc.psum_tensor("bpo", [128, CH], F32) as bpo, \
                     nc.psum_tensor("bpa", [CH, CH], F32) as bpa:
                    DV.wait(nc.tensor)
                    LD.wait(nc.tensor)
                    psum_gate()
                    at_dv = DV.n
                    for c in range(NCH):
                        nc.tensor.wait_ge(DV.sem, at_dv)
                        ptok = PE.inc(nc.tensor.transpose(
                            bpt[:], bat[:, CH * c:CH * (c + 1)], ident[0:KB, 0:KB]))
                        nc.vector.wait_ge(PE.sem, ptok)
                        dv(nc.vector.tensor_copy(attok[:, KB * c:KB * (c + 1)],
                                                 bpt[:]))
                        at_dv = DV.n
                    dv(nc.vector.memset(S_sb[:], 0.0))
                    dv(nc.vector.memset(S_bf[:], 0.0))
                    dv(nc.vector.memset(dpr[:], 0.0))
                    dv(nc.vector.tensor_scalar(dpr[:, 0:1], dpr[:, 0:1], 1.0,
                                               None, ALU.add))
                    dv(nc.vector.memset(loff[:], 0.0))
                    scan_dv = DV.n
                    for c in range(NCH):
                        nc.tensor.wait_ge(DV.sem, scan_dv)
                        nc.tensor.matmul(bpo[:, :], S_bf[:, :],
                                         bqt[:, CH * c:CH * (c + 1)],
                                         start=True, stop=False)
                        nc.tensor.matmul(bpa[:], bat[:, CH * c:CH * (c + 1)],
                                         bqt[:, CH * c:CH * (c + 1)],
                                         start=True, stop=True)
                        t_pe = PE.inc(nc.tensor.matmul(
                            bpu[:], attok[:, KB * c:KB * (c + 1)],
                            vtokb[:, VB * c:VB * (c + 1)], start=True, stop=True))
                        nc.vector.wait_ge(PE.sem, t_pe)
                        dvf(nc.vector.tensor_mul(atm[:], bpa[:], cmask[:]))
                        nc.tensor.wait_ge(DV.sem, DV.n)
                        t_pe2 = PE.inc(nc.tensor.matmul(
                            bpo[:, :], vtokb[:, VB * c:VB * (c + 1)], atm[:],
                            start=False, stop=True))
                        nc.vector.wait_ge(PE.sem, t_pe2)
                        dvf(nc.vector.tensor_add(Stmp[:], S_sb[:], bpu[:]))
                        dvf(nc.vector.tensor_scalar(
                            S_sb[:], Stmp[:], beL[:, CH * (c + 1) - 1:CH * (c + 1)],
                            None, ALU.mult))
                        dvf(nc.vector.tensor_copy(S_bf[:], S_sb[:]))
                        dvf(nc.vector.tensor_mul(dpr[:, c + 1:c + 2], dpr[:, c:c + 1],
                                                 beL[:, CH * (c + 1) - 1:CH * (c + 1)]))
                        dvf(nc.vector.tensor_add(loff[:, c + 1:c + 2],
                                                 loff[:, c:c + 1],
                                                 bL[:, CH * (c + 1) - 1:CH * (c + 1)]))
                        dvf(nc.vector.tensor_copy(o_f32[:, CH * c:CH * (c + 1)],
                                                  bpo[:, :]))
                        scan_dv = DV.n
                DV.wait(nc.sync)
                ST.dinc(nc.sync.dma_start(out=agtr_in[0:KB, :], in_=S_sb[:]))
                ST.dinc(nc.sync.dma_start(
                    out=agtr_in[KB:KB + 1, 0:KB].rearrange("a k -> k a"),
                    in_=dpr[:, NCH:NCH + 1]))
                collective("tr", agtr_in, agtr_out)
                cc_wait("tr", nc.sync)
                for r in range(NCORES):
                    LD.dinc(nc.sync.dma_start(
                        out=u_all[:, VB * r:VB * (r + 1)],
                        in_=agtr_out[(KB + 1) * r:(KB + 1) * r + KB, :]))
                    LD.dinc(nc.sync.dma_start(
                        out=d_all[:, r:r + 1],
                        in_=agtr_out[(KB + 1) * r + KB:(KB + 1) * (r + 1), 0:KB]
                        .ap().rearrange("a k -> k a")
                        if False else
                        agtr_out[(KB + 1) * r + KB:(KB + 1) * (r + 1), 0:KB]
                        .rearrange("a k -> k a")))
                LD.wait(nc.vector)
                dvf(nc.vector.memset(pref[:, 0:VB], 0.0))
                for r in range(NCORES):
                    dvf(nc.vector.tensor_scalar(
                        smy1[:], pref[:, VB * r:VB * (r + 1)], d_all[:, r:r + 1],
                        None, ALU.mult))
                    dvf(nc.vector.tensor_add(
                        pref[:, VB * (r + 1):VB * (r + 2)], smy1[:],
                        u_all[:, VB * r:VB * (r + 1)]))
                dvf(nc.vector.tensor_mul(pref[:, 0:8 * VB], pref[:, 0:8 * VB],
                                         r1hot[:]))
                dvf(nc.vector.tensor_add(smy0[:], pref[:, 0:VB], pref[:, VB:2 * VB]))
                dvf(nc.vector.tensor_add(smy1[:], pref[:, 2 * VB:3 * VB],
                                         pref[:, 3 * VB:4 * VB]))
                dvf(nc.vector.tensor_add(smy0[:], smy0[:], smy1[:]))
                dvf(nc.vector.tensor_add(smy1[:], pref[:, 4 * VB:5 * VB],
                                         pref[:, 5 * VB:6 * VB]))
                dvf(nc.vector.tensor_add(smy1[:], smy1[:], pref[:, 6 * VB:7 * VB]))
                dvf(nc.vector.tensor_add(smy1[:], smy1[:], pref[:, 7 * VB:8 * VB]))
                dvf(nc.vector.tensor_add(smy0[:], smy0[:], smy1[:]))
                for c in range(NCH):
                    DV.wait(nc.scalar)
                    ac(nc.scalar.activation(
                        bt1[:, CH * c:CH * (c + 1)], bL[:, CH * c:CH * (c + 1)],
                        AF.Exp, bias=loff[:, c:c + 1], scale=1.0))
                AC.wait(nc.vector)
                dvf(nc.vector.tensor_mul(bt2[:], bq_f[:], bt1[:]))
                barrier()
                with nc.psum_tensor("bpf", [128, 512], F32) as bpf:
                    fix_dv = DV.n
                    for th in range(2):
                        nc.tensor.wait_ge(DV.sem, fix_dv)
                        if th == 0:
                            psum_gate()
                        ptok = PE.inc(nc.tensor.matmul(
                            bpf[:], smy0[:, :], bt2[:, 512 * th:512 * (th + 1)],
                            start=True, stop=True))
                        nc.vector.wait_ge(PE.sem, ptok)
                        dv(nc.vector.tensor_add(
                            o_f32[:, 512 * th:512 * (th + 1)],
                            o_f32[:, 512 * th:512 * (th + 1)], bpf[:]))
                        fix_dv = DV.n
                if DEBUG:
                    DV.wait(nc.sync)
                    ST.dinc(nc.sync.dma_start(out=dbg_bo[:, :], in_=o_f32[:]))
                    ST.dinc(nc.sync.dma_start(out=dbg_smy[:, :], in_=smy0[:]))
                    ST.dinc(nc.sync.dma_start(out=dbg_tr[:, :], in_=agtr_out[:, :]))
                    ST.dinc(nc.sync.dma_start(out=dbg_blb[:, :], in_=blb_f[:]))
                    ST.dinc(nc.sync.dma_start(out=dbg_bq2[:, :], in_=bq_f[:]))
                dvf(nc.vector.tensor_copy(o_bf[:], o_f32[:]))
                barrier()
                with nc.psum_tensor("bpw", [128, 4096], F32) as bpw:
                    DV.wait(nc.tensor)
                    psum_gate()
                    for tp in range(4):
                        barrier()
                        mm = None
                        for tt in range(2):
                            t8 = 2 * tp + tt
                            for do4 in range(4):
                                mm = nc.tensor.matmul(
                                    bpw[:, 2048 * tt + 512 * do4:
                                        2048 * tt + 512 * (do4 + 1)],
                                    o_bf[:, 128 * t8:128 * (t8 + 1)],
                                    bwo_sb[:, 512 * do4:512 * (do4 + 1)],
                                    start=True, stop=True)
                        ptok = PE.inc(mm)
                        nc.vector.wait_ge(PE.sem, ptok)
                        for tt in range(2):
                            t8 = 2 * tp + tt
                            xt = x_sb[:, t8 * D:(t8 + 1) * D]
                            dv(nc.vector.tensor_add(
                                xt, xt, bpw[:, 2048 * tt:2048 * (tt + 1)]))

        if STAGE >= 4:
            emit_ffn()

        barrier()
        DV.wait(nc.sync)
        AC.wait(nc.sync)
        PE.wait(nc.sync)
        if DEBUG:
            for (dst, srct) in [(dbg_xn, xn_tok_d), (dbg_q, q_d), (dbg_k, k_d),
                                (dbg_v, v_d), (dbg_z, agz_in), (dbg_zl, zloc_d),
                                (dbg_wqt, wqT_d)]:
                ST.dinc(nc.sync.dma_start(out=dst[:, :], in_=srct[:, :]))
            ST.dinc(nc.sync.dma_start(out=dbg_nb[:, :], in_=negb_sb[:]))
            ST.dinc(nc.sync.dma_start(out=dbg_st[:, :], in_=dbgst_sb[:]))
            ST.dinc(nc.sync.dma_start(out=dbg_wo[:, :], in_=agwo_out[0:256, :]))
            ST.dinc(nc.sync.dma_start(out=dbg_wob[:, :], in_=wo_bf_d[:, :]))
            ST.dinc(nc.sync.dma_start(out=dbg_woi[:, :], in_=agwo_in[:, :]))
            if STAGE >= 2:
                ST.dinc(nc.sync.dma_start(out=dbg_gu[:, :], in_=aggu_out[0:256, :]))
                ST.dinc(nc.sync.dma_start(out=dbg_gui[:, :], in_=aggu_in[0:256, :]))
                ST.dinc(nc.sync.dma_start(out=dbg_dw[:, :], in_=agdw_out[0:256, :]))
                ST.dinc(nc.sync.dma_start(out=dbg_h[:, :], in_=h_d[:, :]))
        for t in range(8):
            ST.dinc(nc.sync.dma_start(out=y_out[128 * t:128 * (t + 1), :],
                                      in_=x_sb[:, t * D:(t + 1) * D]))
        ST.wait(nc.sync)
    return nc


def _consts(rank):
    smask = np.zeros((128, 1024), dtype=np.float32)
    for u in range(128):
        smask[u, :u + 512] = MASKVAL
    cmask = np.zeros((CH, CH), dtype=ml_dtypes.bfloat16)
    for u in range(CH):
        cmask[u, u:] = 1.0
    r1hot = np.zeros((KB, 8 * VB), dtype=np.float32)
    r1hot[:, VB * rank:VB * (rank + 1)] = 1.0
    ind = np.zeros((1, 512), dtype=np.float32)
    for g in range(4):
        ind[0, 128 * g + 32 * g:128 * g + 32 * (g + 1)] = 1.0
    return dict(
        smask_in=smask, cmask_in=cmask, r1hot_in=r1hot,
        rk_in=np.array([[rank]], dtype=np.int32),
        onesc_in=np.ones((128, 1), dtype=np.float32),
        onesb_in=np.ones((128, 1), dtype=ml_dtypes.bfloat16),
        onesr_in=np.ones((1, 128), dtype=np.float32),
        ind_in=ind,
        ident_in=np.eye(128, dtype=ml_dtypes.bfloat16),
        eps_in=np.tile(np.array([[1e-5, 1e-6]], dtype=np.float32), (128, 1)),
    )


def kernel(L0_attn_a_x, L0_attn_a_Wq, L0_attn_a_Wk, L0_attn_a_Wv, L0_attn_a_Wo,
           L0_attn_b_Wq, L0_attn_b_Wk, L0_attn_b_Wv, L0_attn_b_Wo, L0_attn_b_Wb,
           L0_attn_b_bb, L0_attn_b_Wd, L0_attn_b_bd,
           L0_ffn_gate_up_packed, L0_ffn_gate_up_scales,
           L0_ffn_down_packed, L0_ffn_down_scales):
    x = np.ascontiguousarray(np.asarray(L0_attn_a_x, dtype=np.float32)[0])
    f32 = lambda a: np.ascontiguousarray(np.asarray(a, dtype=np.float32))
    wq, wk, wv, wo = map(f32, (L0_attn_a_Wq, L0_attn_a_Wk, L0_attn_a_Wv, L0_attn_a_Wo))
    gup = np.ascontiguousarray(np.asarray(L0_ffn_gate_up_packed, dtype=np.int32))
    gsc = np.ascontiguousarray(np.asarray(L0_ffn_gate_up_scales, dtype=np.float16))
    dwp = np.ascontiguousarray(np.asarray(L0_ffn_down_packed, dtype=np.int32))
    dsc = np.ascontiguousarray(np.asarray(L0_ffn_down_scales, dtype=np.float16))
    bq, bk, bv, bo, bb_w, bd_w = map(
        f32, (L0_attn_b_Wq, L0_attn_b_Wk, L0_attn_b_Wv, L0_attn_b_Wo,
              L0_attn_b_Wb, L0_attn_b_Wd))
    bbbv = f32(L0_attn_b_bb).reshape(KB, 1)
    bbdv = f32(L0_attn_b_bd).reshape(KB, 1)

    nc = build_program()
    in_maps = []
    for c in range(NCORES):
        m = dict(
            x_in=np.ascontiguousarray(x[TL * c:TL * (c + 1)]),
            wq_s=np.ascontiguousarray(wq[256 * c:256 * (c + 1)].T),
            wk_s=np.ascontiguousarray(wk[256 * c:256 * (c + 1)].T),
            wv_s=np.ascontiguousarray(wv[256 * c:256 * (c + 1)].T),
            wo_s=np.ascontiguousarray(wo[:, 256 * c:256 * (c + 1)].T),
            gup_s=np.ascontiguousarray(
                gup[1408 * c:1408 * (c + 1)].view(np.uint16).reshape(1408, 512).T),
            gup_sc=np.ascontiguousarray(gsc[:, 1408 * c:1408 * (c + 1)]),
            dwn_s=np.ascontiguousarray(
                dwp[256 * c:256 * (c + 1)].view(np.uint16).reshape(256, 1408).T),
            dwn_sc=np.ascontiguousarray(dsc[:, 256 * c:256 * (c + 1)]),
            bwq=bq, bwk=bk, bwv=bv, bwo=bo, bwb=bb_w, bwd=bd_w,
            bbb=bbbv, bbd=bbdv,
        )
        m.update(_consts(c))
        in_maps.append(m)
    res = run_bass_kernel_spmd(nc, in_maps, core_ids=list(range(NCORES)))
    out = np.concatenate([res.results[c]["y_out"] for c in range(NCORES)], axis=0)
    return out.reshape(1, T, D)
